# version 12 (final)
# baseline (speedup 1.0000x reference)
import numpy as np
import ml_dtypes
import concourse.bass as bass
import concourse.mybir as mybir
from concourse.bass_utils import run_bass_kernel_spmd

F32 = mybir.dt.float32
BF16 = mybir.dt.bfloat16
I16 = mybir.dt.int16
U16 = mybir.dt.uint16
AF = mybir.ActivationFunctionType
ALU = mybir.AluOpType
AX = mybir.AxisListType

B = 2
NTOK = 8192
E = 1024
I3 = 3072
TOWN = 2048
NGR = 16
NKT = 64
NQT = 32
QMAXF = 32767.0
LN_EPS = 1e-5
LAM_INIT = float(0.8 - 0.6 * np.exp(-0.3 * 12))
BF = ml_dtypes.bfloat16

A_AMX = 0
A_AMUP = 1
A_AMG = 2
A_AMP1 = 3
A_AMP2 = 4
A_NORM = [5, 6]
A_GATH = 24
A_TMP = [8, 16, 26]


class KB:
    def __init__(self, debug=False, tapmask=None):
        self.debug = debug
        self.tapmask = tapmask
        nc = bass.Bass("TRN2", target_bir_lowering=False, debug=False)
        self.nc = nc
        self._ctx = []
        self.ops = {e: [] for e in ("sync", "gpsimd", "tensor", "scalar", "vector")}
        self.semh = {}
        self.semn = {}
        self.lastw = {}
        self.trace = []
        for s in ("dA", "dB", "T", "S", "V", "C"):
            g = nc.semaphore(s)
            self.semh[s] = g.__enter__()
            self._ctx.append(g)
            self.semn[s] = 0
        self.declare_dram()
        self.declare_sbuf()
        self.tapn = 0
        self.tapnames = []

    def emit(self, eng, fn):
        self.ops[eng].append(fn)

    def inc(self, sem, k=1):
        self.semn[sem] += k
        return self.semn[sem]

    def wait(self, eng, sem, val):
        if val is None or val <= 0:
            return
        key = (eng, sem)
        if self.lastw.get(key, 0) >= val:
            return
        self.lastw[key] = val
        self.trace.append(("wait", eng, sem, val))
        h = self.semh[sem]
        self.emit(eng, lambda e, h=h, v=val: e.wait_ge(h, v))

    def w(self, eng, tok):
        if tok is not None:
            self.wait(eng, tok[0], tok[1])

    def dma(self, eng, out, in_, sem=None):
        if sem is None:
            sem = "dA" if eng == "sync" else "dB"
        v = self.inc(sem, 16)
        self.trace.append(("inc", eng, sem, 16))
        h = self.semh[sem]
        self.emit(eng, lambda e, o=out, i=in_, h=h: e.dma_start(out=o, in_=i).then_inc(h, 16))
        return (sem, v)

    def op(self, eng, fn, sem=None):
        if sem is None:
            sem = {"tensor": "T", "scalar": "S", "vector": "V", "gpsimd": "C"}[eng]
        v = self.inc(sem, 1)
        self.trace.append(("inc", eng, sem, 1))
        h = self.semh[sem]
        self.emit(eng, lambda e, f=fn, h=h: f(e).then_inc(h, 1))
        return (sem, v)

    def opn(self, eng, fn):
        self.emit(eng, fn)

    def tok(self, eng):
        sem = {"tensor": "T", "scalar": "S", "vector": "V", "gpsimd": "C",
               "sync": "dA"}.get(eng, eng)
        return (sem, self.semn[sem])

    def alloc(self, name, shape, dt):
        g = self.nc.sbuf_tensor(name, shape, dt)
        t = g.__enter__()
        self._ctx.append(g)
        return t

    def palloc(self, name, shape, dt):
        g = self.nc.psum_tensor(name, shape, dt)
        t = g.__enter__()
        self._ctx.append(g)
        return t

    def declare_dram(self):
        nc = self.nc
        self.dx_full = nc.dram_tensor("x_full", [NTOK, E], F32, kind="ExternalInput")
        self.dx_own = nc.dram_tensor("x_own", [TOWN, E], F32, kind="ExternalInput")
        self.dwq = nc.dram_tensor("wq_sl", [E, 256], F32, kind="ExternalInput")
        self.dwk = nc.dram_tensor("wk_sl", [E, 256], F32, kind="ExternalInput")
        self.dwv = nc.dram_tensor("wv_sl", [E, 256], F32, kind="ExternalInput")
        self.dwo = nc.dram_tensor("wo_sl", [256, E], F32, kind="ExternalInput")
        self.dwgate = nc.dram_tensor("w_gate", [I3, E], F32, kind="ExternalInput")
        self.dwup = nc.dram_tensor("w_up", [I3, E], F32, kind="ExternalInput")
        self.dwdown = nc.dram_tensor("w_down", [E, I3], F32, kind="ExternalInput")
        self.dlv = nc.dram_tensor("lvecs", [1, 256], F32, kind="ExternalInput")
        self.dlng = nc.dram_tensor("ln_g", [1, 128], F32, kind="ExternalInput")
        self.dlnb = nc.dram_tensor("ln_b", [1, 128], F32, kind="ExternalInput")
        self.dout = nc.dram_tensor("out_own", [TOWN, E], F32, kind="ExternalOutput")

        self.d_xn_own = nc.dram_tensor("d_xn_own", [TOWN, E], F32)
        self.d_wupT = nc.dram_tensor("d_wupT", [24, 8, 128, 128], BF16)
        self.d_wgateT = nc.dram_tensor("d_wgateT", [24, 8, 128, 128], BF16)
        self.d_wdownT = nc.dram_tensor("d_wdownT", [24, 128, E], BF16)
        self.d_up = nc.dram_tensor("d_up", [24, 128, TOWN], F32)
        self.d_gate = nc.dram_tensor("d_gate", [24, 128, TOWN], F32)
        self.d_prod = nc.dram_tensor("d_prod", [24, 128, TOWN], F32)
        self.d_iu = nc.dram_tensor("d_iu", [24, 128, TOWN], I16)
        self.d_it = nc.dram_tensor("d_it", [24, 128, TOWN], I16)
        self.d_oq = nc.dram_tensor("d_oq", [24, 128, TOWN], BF16)
        self.d_cc_in = [nc.dram_tensor(f"d_cc{i}_in", [4, 1], F32) for i in range(4)]
        self.d_cc_out = [nc.dram_tensor(f"d_cc{i}_out", [4, 1], F32) for i in range(4)]
        self.d_rs_in = nc.dram_tensor("d_rs_in", [NTOK, E], BF16)
        self.d_rs_out = nc.dram_tensor("d_rs_out", [TOWN, E], BF16)
        self.d_identb = nc.inline_tensor(np.eye(128, dtype=BF), "c_identb")
        self.d_identf = nc.inline_tensor(np.eye(128, dtype=np.float32), "c_identf")
        self.d_onesr = nc.inline_tensor(np.ones((1, 128), np.float32), "c_onesr")
        self.d_ones128b = nc.inline_tensor(np.ones((128, 128), dtype=BF), "c_ones128b")
        if self.debug:
            self.d_tap = nc.dram_tensor("d_tap", [32, 128, 2048], F32,
                                        kind="ExternalOutput")

    def declare_sbuf(self):
        a = self.alloc
        self.identb = a("identb", [128, 128], BF16)
        self.identf = a("identf", [128, 128], F32)
        self.onesr = a("onesr", [1, 128], F32)
        self.ones128b = a("ones128b", [128, 128], BF16)
        self.LNG = a("LNG", [128, 128], F32)
        self.LNB = a("LNB", [128, 128], F32)
        self.SC = a("SC", [128, 16], F32)
        self.lrow = a("lrow", [1, 256], F32)
        self.lnrow = a("lnrow", [1, 256], F32)
        self.srow = a("srow", [1, 24], F32)
        self.red8 = a("red8", [8, 4], F32)
        self.acc = a("acc", [128, 32], F32)
        self.ccout = a("ccout", [8, 4], F32)
        self.wq_sb = a("wq_sb", [128, 2048], BF16)
        self.wk_sb = a("wk_sb", [128, 2048], BF16)
        self.wv_sb = a("wv_sb", [128, 2048], BF16)
        self.wo_sb = a("wo_sb", [128, 2048], BF16)
        self.AR_A = a("AR_A", [128, 49280], BF16)
        A = self.AR_A.ap()
        self.KF = [A[:, 0:8192], A[:, 8192:16384]]
        self.QF = [A[:, 16384:24576], A[:, 24576:32768]]
        self.VA = [A[:, 32768:41024], A[:, 41024:49280]]
        self.OFC = [A[:, i * 8192:(i + 1) * 8192] for i in range(4)]
        self.AR_B = a("AR_B", [128, 20480], F32)
        bb = self.AR_B.ap()
        self.b_f32 = lambda c0, n: bb[:, c0:c0 + n]
        self.b_bf = lambda c0, n: bb[:, c0:c0 + n].bitcast(BF16)
        self.b_i16 = lambda c0, n: bb[:, c0:c0 + n].bitcast(I16)
        self.b_u16 = lambda c0, n: bb[:, c0:c0 + n].bitcast(U16)
        self.wnat = a("wnat", [128, 1024], F32)
        self.wnat2 = a("wnat2", [128, 640], F32)
        self.wq16 = a("wq16", [128, 1024], BF16)
        self.wsc = a("wsc", [128, 64], F32)
        self.wT_st = a("wT_st", [128, 1024], BF16)
        self.ps_s = self.palloc("ps_s", [128, 2048], F32)
        self.ps_o = self.palloc("ps_o", [128, 1024], F32)
        self.ps_g = self.palloc("ps_g", [128, 512], F32)
        self.ps_t = self.palloc("ps_t", [128, 1024], BF16)
        S = self.ps_s.ap()
        self.SREG = [S[:, 0:1024], S[:, 1024:2048]]
        O = self.ps_o.ap()
        self.O_AP = [[O[:, 0:129], O[:, 129:258]], [O[:, 512:641], O[:, 641:770]]]

    SLOT_NEGM = (0, 1)
    SLOT_NEGLAM = 2
    SLOT_RX = 3
    SLOT_SX = 4
    SLOT_RU = 5
    SLOT_RG = 6
    SLOT_SG = 7
    SLOT_C3 = 8
    SLOT_C4 = 9
    SLOT_SO = 10

    def sc(self, slot):
        return self.SC.ap()[:, slot:slot + 1]

    def sr(self, i):
        return self.srow.ap()[0:1, i:i + 1]

    def ac(self, i):
        return self.acc.ap()[:, i:i + 1]

    def tap(self, ap, tok=None, name=None):
        if not self.debug:
            return
        if self.tapmask is not None and name not in self.tapmask:
            return
        n = self.tapn
        self.tapn += 1
        self.tapnames.append(name or f"tap{n}")
        if tok is not None:
            self.w("sync", tok)
        pdim, fdim = ap.shape[0], ap.shape[-1]
        dst = self.d_tap.ap()[n]
        sdt = ap.dtype
        if sdt != F32:
            dst = dst.bitcast(sdt)
        self.dma("sync", dst[0:pdim, 0:fdim], ap)
        return n

    def p0(self):
        d = self.dma
        self.ld_identb = d("sync", self.identb[:, :], self.d_identb.ap())
        self.ld_identf = d("sync", self.identf[:, :], self.d_identf.ap())
        self.ld_onesr = d("sync", self.onesr[:, :], self.d_onesr.ap())
        self.ld_ones128b = d("sync", self.ones128b[:, :], self.d_ones128b.ap())
        self.ld_lrow = d("sync", self.lrow[:, :], self.dlv.ap())
        self.ld_lnrow = d("sync", self.lnrow[0:1, 0:128], self.dlng.ap())
        self.ld_lnrow2 = d("sync", self.lnrow[0:1, 128:256], self.dlnb.ap())

        def wload(dst_sb, src_dram, wid):
            return d("gpsimd",
                     dst_sb.ap().rearrange("p (c w) -> p c w", w=wid),
                     src_dram.ap().rearrange("(c p) w -> p c w", p=128))
        self.ld_wq = wload(self.wq_sb, self.dwq, 256)
        self.ld_wk = wload(self.wk_sb, self.dwk, 256)
        self.ld_wv = wload(self.wv_sb, self.dwv, 256)
        self.ld_wo = wload(self.wo_sb, self.dwo, 1024)

        self.psg_free = None
        self.wait("tensor", "dA", max(self.ld_onesr[1], self.ld_lnrow2[1]))
        t = self.op("tensor", lambda e: e.matmul(
            self.ps_g[:, 0:256], self.onesr[0:1, :], self.lnrow[0:1, :],
            start=True, stop=True))
        self.w("vector", t)
        ev1 = self.op("vector", lambda e: e.tensor_copy(self.LNG[:, :], self.ps_g[:, 0:128]))
        ev2 = self.op("vector", lambda e: e.tensor_copy(self.LNB[:, :], self.ps_g[:, 128:256]))
        self.psg_free = ev2

        self.lscr = self.lnrow.ap()[0:1, 0:64]
        self.wait("vector", "dA", self.ld_lrow[1])
        self.wait("vector", "T", self.semn["T"])
        lr = self.lrow.ap()
        sr = self.sr
        v1 = self.op("vector", lambda e: e.tensor_mul(self.lscr, lr[0:1, 0:64], lr[0:1, 64:128]))
        self.w("vector", v1)
        v2 = self.op("vector", lambda e: e.reduce_sum(sr(0), self.lscr, axis=AX.X))
        v3 = self.op("vector", lambda e: e.tensor_mul(self.lscr, lr[0:1, 128:192], lr[0:1, 192:256]))
        self.w("vector", v3)
        v4 = self.op("vector", lambda e: e.reduce_sum(sr(1), self.lscr, axis=AX.X))
        self.w("scalar", v2)
        self.w("scalar", v4)
        s1 = self.op("scalar", lambda e: e.activation(sr(2), sr(0), AF.Exp))
        s2 = self.op("scalar", lambda e: e.activation(sr(3), sr(1), AF.Exp))
        self.w("vector", s1)
        self.w("vector", s2)
        v5 = self.op("vector", lambda e: e.tensor_sub(sr(4), sr(3), sr(2)))
        self.w("vector", v5)
        self.neglam_done = self.op("vector", lambda e: e.tensor_scalar_add(sr(5), sr(4), -LAM_INIT))

    def emit_ln(self, b, xt, scr, out_bf, out_f32, am_col, scr_war_tok):
        c = A_TMP[b]
        ac = self.ac
        self.w("scalar", scr_war_tok)
        t_sq = self.op("scalar", lambda e: e.activation(scr, xt, AF.Square,
                                                        accum_out=ac(c)))
        t_sum = self.op("vector", lambda e: e.reduce_sum(ac(c + 1), xt, axis=AX.X))
        self.w("vector", t_sum)
        t_m = self.op("vector", lambda e: e.tensor_scalar_mul(ac(c + 2), ac(c + 1), -1.0 / E))
        self.w("vector", t_sq)
        t_v1 = self.op("vector", lambda e: e.tensor_scalar_mul(ac(c + 3), ac(c), 1.0 / E))
        self.w("vector", t_m)
        t_v2 = self.op("vector", lambda e: e.tensor_mul(ac(c + 4), ac(c + 2), ac(c + 2)))
        self.w("vector", t_v2)
        t_v3 = self.op("vector", lambda e: e.tensor_sub(ac(c + 3), ac(c + 3), ac(c + 4)))
        self.w("vector", t_v3)
        t_v4 = self.op("vector", lambda e: e.tensor_scalar_add(ac(c + 3), ac(c + 3), LN_EPS))
        self.w("scalar", t_v4)
        t_ln = self.op("scalar", lambda e: e.activation(ac(c + 4), ac(c + 3), AF.Ln))
        self.w("scalar", t_ln)
        t_ex = self.op("scalar", lambda e: e.activation(ac(c + 5), ac(c + 4), AF.Exp, scale=-0.5))
        self.w("vector", t_ex)
        t_mm = self.op("vector", lambda e: e.tensor_mul(ac(c + 4), ac(c + 2), ac(c + 5)))
        self.w("vector", t_mm)
        out = {"sq": t_sq}
        if out_bf is not None:
            out["bf"] = self.op("vector", lambda e: e.tensor_scalar(
                out_bf, xt, ac(c + 5), ac(c + 4), ALU.mult, ALU.add))
        if out_f32 is not None:
            out["f32"] = self.op("vector", lambda e: e.tensor_scalar(
                out_f32, xt, ac(c + 5), ac(c + 4), ALU.mult, ALU.add))
            if am_col is not None:
                self.w("vector", out["f32"])
                t_am = self.op("vector", lambda e: e.reduce_max(
                    ac(c + 4), out_f32, axis=AX.X, apply_absolute_value=True))
                self.w("vector", t_am)
                out["am"] = self.op("vector", lambda e: e.tensor_tensor(
                    ac(am_col), ac(c + 4), ac(am_col), ALU.max))
        out["xt_done"] = self.tok("vector")
        return out

    def p1_own_step(self, i):
        XT = self.b_f32(10240, 1024)
        SCR = self.b_f32(11264, 1024)
        XNO = self.b_f32(12288, 1024)
        st = self.ostate
        self.w("gpsimd", st.get("xt_done"))
        self.w("gpsimd", st.get("sq"))
        ld = self.dma("gpsimd", XT, self.dx_own.ap()[i * 128:(i + 1) * 128, :])
        self.wait("scalar", ld[0], ld[1])
        self.wait("vector", ld[0], ld[1])
        if st.get("store") is not None:
            self.w("vector", st["store"])
        toks = self.emit_ln(2, XT, SCR, None, XNO, A_AMX, st.get("xt_done"))
        self.w("gpsimd", toks["f32"])
        stt = self.dma("gpsimd", self.d_xn_own.ap()[i * 128:(i + 1) * 128, :], XNO)
        self.ostate = dict(store=stt, xt_done=toks["xt_done"], sq=toks["sq"])
        self.xn_own_stored = stt

    def cc_all_reduce_max(self, idx, acc_cols):
        k = len(acc_cols)
        for r, col in enumerate(acc_cols):
            t = self.op("vector", lambda e, col=col, r=r: e.tensor_copy(
                self.ac(A_GATH + r), self.ac(col)))
        self.wait("tensor", "V", self.semn["V"])
        self.wait("tensor", "dA", self.ld_identf[1])
        self.w("tensor", self.psg_free)
        tt = self.op("tensor", lambda e: e.transpose(
            self.ps_g[0:4, 0:128], self.acc[:, A_GATH:A_GATH + 4], self.identf[:, :]))
        self.w("vector", tt)
        rv = self.op("vector", lambda e: e.reduce_max(
            self.red8[0:4, 0:1], self.ps_g[0:4, 0:128], axis=AX.X))
        self.psg_free = rv
        self.w("sync", rv)
        st = self.dma("sync", self.d_cc_in[idx].ap()[0:4, :], self.red8[0:4, 0:1])
        self.wait("gpsimd", st[0], st[1])
        cin = self.d_cc_in[idx].ap()
        cout = self.d_cc_out[idx].ap()
        cc = self.op("gpsimd", lambda e: e.collective_compute(
            "AllReduce", ALU.max, replica_groups=[list(range(8))],
            ins=[cin.opt()], outs=[cout.opt()]), sem="C")
        self.wait("sync", "C", cc[1])
        self.wait("sync", "V", self.semn["V"])
        ldo = self.dma("sync", self.ccout[0:4, 0:1], self.d_cc_out[idx].ap())
        return ldo

    def ccout_to_srow(self, dst_sr_base, ldo=None, k=1):
        if ldo is not None:
            self.wait("tensor", ldo[0], ldo[1])
        self.w("tensor", self.psg_free)
        tt = self.op("tensor", lambda e: e.transpose(
            self.ps_g[0:1, 0:8], self.ccout[0:8, 0:1], self.identf[0:8, 0:8]))
        self.w("vector", tt)
        g = self.op("vector", lambda e, d=dst_sr_base, k=k: e.tensor_copy(
            self.srow[0:1, d:d + k], self.ps_g[0:1, 0:k]))
        self.psg_free = g
        self.w("vector", g)
        return g

    def bcast_row(self, row_ap, width, dst_col0):
        self.wait("tensor", "V", self.semn["V"])
        self.w("tensor", self.psg_free)
        t = self.op("tensor", lambda e: e.matmul(
            self.ps_g[:, 0:width], self.onesr[0:1, :], row_ap, start=True, stop=True))
        self.w("vector", t)
        ev = self.op("vector", lambda e: e.tensor_copy(
            self.SC[:, dst_col0:dst_col0 + width], self.ps_g[:, 0:width]))
        self.psg_free = ev
        return ev

    def cc0_scalars(self):
        self.wait("vector", self.ldo_cc0[0], self.ldo_cc0[1])
        g = self.ccout_to_srow(8, ldo=self.ldo_cc0)
        sr = self.sr
        v1 = self.op("vector", lambda e: e.tensor_scalar(
            sr(16 + self.SLOT_SX), sr(8), 1.0 / QMAXF, 1e-12, ALU.mult, ALU.max))
        self.w("vector", v1)
        v2 = self.op("vector", lambda e: e.reciprocal(
            sr(16 + self.SLOT_RX), sr(16 + self.SLOT_SX)))
        self.w("vector", self.neglam_done)
        v3 = self.op("vector", lambda e: e.tensor_copy(sr(16 + self.SLOT_NEGLAM), sr(5)))
        self.w("vector", v2)
        self.w("vector", v3)
        self.sc_x_ready = self.bcast_row(self.srow[0:1, 18:21], 3, self.SLOT_NEGLAM)

    def p1_full(self):
        XT = [self.b_f32(0, 1024), self.b_f32(1024, 1024)]
        SCR = [self.b_f32(2048, 1024), self.b_f32(3072, 1024)]
        XNB = [self.b_bf(4096, 512), self.b_bf(4608, 512)]
        XNF = [self.b_bf(6144, 2048), self.b_bf(8192, 2048)]
        self.wait("tensor", "dA", self.ld_identb[1])
        self.wait("tensor", "dB", max(self.ld_wq[1], self.ld_wk[1], self.ld_wv[1]))
        z = self.op("vector", lambda e: e.memset(self.acc[:, :], 0.0))
        self.w("vector", z)
        self.w("scalar", z)
        self.ostate = {}

        pst = self.ps_t.ap()
        prev = [dict(xt_done=None, sq=None), dict(xt_done=None, sq=None)]
        prev_xnb_use = [None, None]
        prev_pst_ev = None
        prev_xnf_use = [None, None]
        prev_qk_ev = None
        prev_v_ev = [None, None]
        self.kf_written = []
        self.va_written = []

        for h in range(2):
            va = self.VA[h].rearrange("p (k c) -> p k c", c=129)[:, :, 128:129]
            t = self.op("vector", lambda e, va=va: e.memset(va, 1.0))
            self.va_ones = t

        for g in range(NGR):
            gb = g % 2
            xnf = XNF[gb]
            self.p1_own_step(g)
            ev_toks = []
            for s in range(4):
                i = g * 4 + s
                b = s % 2
                self.w("sync", prev[b]["xt_done"])
                self.w("sync", prev[b]["sq"])
                ld = self.dma("sync", XT[b], self.dx_full.ap()[i * 128:(i + 1) * 128, :])
                self.wait("scalar", "dA", ld[1])
                self.wait("vector", "dA", ld[1])
                if prev_xnb_use[b] is not None:
                    self.w("vector", prev_xnb_use[b])
                toks = self.emit_ln(b, XT[b], SCR[b], XNB[b], None, None,
                                    prev[b]["sq"])
                prev[b] = dict(xt_done=toks["xt_done"], sq=toks["sq"])
                self.w("tensor", toks["bf"])
                if prev_pst_ev is not None:
                    self.w("tensor", prev_pst_ev)
                xnb = XNB[b]
                for e8 in range(8):
                    fn = (lambda e, xnb=xnb, e8=e8: e.transpose(
                        pst[:, e8 * 128:(e8 + 1) * 128],
                        xnb[:, e8 * 128:(e8 + 1) * 128], self.identb[:, :]))
                    if e8 == 7:
                        t_tr = self.op("tensor", fn)
                    else:
                        self.opn("tensor", fn)
                prev_xnb_use[b] = t_tr
                self.w("vector", t_tr)
                if prev_xnf_use[gb] is not None:
                    self.wait("vector", prev_xnf_use[gb][0], prev_xnf_use[gb][1])
                dst = xnf.rearrange("p (c t) -> p c t", c=8)[:, :, s * 128:(s + 1) * 128]
                src = pst.rearrange("p (c t) -> p c t", c=8)
                ev = self.op("vector", lambda e, dst=dst, src=src: e.tensor_copy(dst, src))
                prev_pst_ev = ev
                ev_toks.append(ev)

            for t in ev_toks:
                self.w("tensor", t)
            if prev_qk_ev is not None:
                self.w("tensor", prev_qk_ev)
            Sget = self.ps_s.ap()
            for h in range(2):
                for iqk, wsb in ((0, self.wq_sb), (1, self.wk_sb)):
                    bank = Sget[:, (iqk * 2 + h) * 512:(iqk * 2 + h) * 512 + 512]
                    for e8 in range(8):
                        lhsT = wsb.ap()[:, e8 * 256 + h * 128: e8 * 256 + h * 128 + 128]
                        rhs = xnf.rearrange("p (c t) -> p c t", c=8)[:, e8, :]
                        fn = (lambda e, bank=bank, lhsT=lhsT, rhs=rhs, e8=e8:
                              e.matmul(bank, lhsT, rhs, start=(e8 == 0), stop=(e8 == 7)))
                        if e8 == 7:
                            t_mm = self.op("tensor", fn)
                        else:
                            self.opn("tensor", fn)
            qk_done = t_mm
            self.w("scalar", qk_done)
            self.w("vector", qk_done)
            evs = []
            for h in range(2):
                qdst = self.QF[h][:, g * 512:(g + 1) * 512]
                ksrc = Sget[:, (2 + h) * 512:(2 + h) * 512 + 512]
                qsrc = Sget[:, h * 512:h * 512 + 512]
                e1 = self.op("scalar", lambda e, d=qdst, s=qsrc: e.activation(
                    d, s, AF.Copy, scale=0.125))
                kdst = self.KF[h][:, g * 512:(g + 1) * 512]
                e2 = self.op("vector", lambda e, d=kdst, s=ksrc: e.tensor_copy(d, s))
                evs += [e1, e2]
            prev_qk_ev = evs[-1]
            self.w("tensor", evs[0])
            self.w("tensor", evs[1])
            self.w("tensor", evs[2])
            self.w("tensor", evs[3])

            for s in range(4):
                i = g * 4 + s
                ob = s % 2
                bank = self.ps_o.ap()[:, ob * 512:ob * 512 + 256]
                if prev_v_ev[ob] is not None:
                    self.w("tensor", prev_v_ev[ob])
                for e8 in range(8):
                    lhsT = xnf.rearrange("p (c t) -> p c t", c=8)[:, e8, s * 128:(s + 1) * 128]
                    rhs = self.wv_sb.ap()[:, e8 * 256:(e8 + 1) * 256]
                    fn = (lambda e, bank=bank, lhsT=lhsT, rhs=rhs, e8=e8:
                          e.matmul(bank, lhsT, rhs, start=(e8 == 0), stop=(e8 == 7)))
                    if e8 == 7:
                        t_v = self.op("tensor", fn)
                    else:
                        self.opn("tensor", fn)
                self.w("vector", t_v)
                self.w("vector", self.va_ones)
                e_list = []
                for h in range(2):
                    vdst = self.VA[h][:, i * 129:i * 129 + 128]
                    vsrc = bank[:, h * 128:(h + 1) * 128] if False else \
                        self.ps_o.ap()[:, ob * 512 + h * 128: ob * 512 + h * 128 + 128]
                    e_v = self.op("vector", lambda e, d=vdst, s=vsrc: e.tensor_copy(d, s))
                    e_list.append(e_v)
                prev_v_ev[ob] = e_list[-1]
            prev_xnf_use[gb] = self.tok("tensor")
            if g == 0:
                self.tap(xnf[:, 0:2048], tok=ev_toks[-1], name="xnf_g0")
                self.tap(self.QF[0][:, 0:512], tok=evs[0], name="qf0")
                self.tap(self.KF[0][:, 0:512], tok=evs[1], name="kf0")
                self.tap(self.VA[0][:, 0:516], tok=e_list[-1], name="va0")

        self.qf_done = self.tok("scalar")
        self.kf_done = self.tok("vector")
        self.proj_done_T = self.tok("tensor")
        self.ldo_cc0 = self.cc_all_reduce_max(0, [A_AMX])

    def p1_dequant(self):
        iwt = self.wnat2.ap()[:, 0:512].bitcast(I16)
        W = self.wsc.ap()
        pst = self.ps_t.ap()
        prev_war = {"wnat": self.tok("vector"), "iwt": self.tok("vector"),
                    "wq16": self.tok("tensor"), "pst": self.tok("vector"),
                    "st": self.tok("dA")}

        self._deq_n = 0
        WN = [self.b_f32(14336, 1024), self.b_f32(15360, 1024)]
        prev_wn = [self.tok("vector"), self.tok("vector")]

        def one(src_rows, dst_kind, dst_idx):
            p = self._deq_n % 2
            self._deq_n += 1
            wn = WN[p]
            self.w("sync", prev_wn[p])
            ld = self.dma("sync", wn, src_rows)
            self.wait("vector", ld[0], ld[1])
            self.w("vector", prev_war["iwt"])
            v1 = self.op("vector", lambda e: e.reduce_max(
                W[:, 0:32], wn.rearrange("p (n w) -> p n w", w=32),
                axis=AX.X, apply_absolute_value=True))
            self.w("vector", v1)
            v2 = self.op("vector", lambda e: e.tensor_scalar(
                W[:, 32:64], W[:, 0:32], 1.0 / 7.0, 1e-12, ALU.mult, ALU.max))
            self.w("vector", v2)
            v3 = self.op("vector", lambda e: e.reciprocal(W[:, 0:32], W[:, 32:64]))
            self.w("vector", v3)
            rb = W[:, 0:32].rearrange("p (n o) -> p n o", o=1).broadcast_to([128, 32, 32])
            sbr = W[:, 32:64].rearrange("p (n o) -> p n o", o=1).broadcast_to([128, 32, 32])
            wv = wn.rearrange("p (n w) -> p n w", w=32)
            v4 = self.op("vector", lambda e: e.tensor_tensor(
                iwt.rearrange("p (n w) -> p n w", w=32), wv, rb, ALU.mult))
            self.w("vector", v4)
            self.w("vector", prev_war["wq16"])
            v5 = self.op("vector", lambda e: e.tensor_tensor(
                self.wq16.ap().rearrange("p (n w) -> p n w", w=32),
                iwt.rearrange("p (n w) -> p n w", w=32), sbr, ALU.mult))
            prev_war["iwt"] = v5
            self.w("tensor", v5)
            self.w("tensor", prev_war["pst"])
            for e8 in range(8):
                fn = (lambda e, e8=e8: e.transpose(
                    pst[:, e8 * 128:(e8 + 1) * 128],
                    self.wq16.ap()[:, e8 * 128:(e8 + 1) * 128], self.identb[:, :]))
                if e8 == 7:
                    t_tr = self.op("tensor", fn)
                else:
                    self.opn("tensor", fn)
            prev_war["wq16"] = t_tr
            self.w("vector", t_tr)
            self.w("vector", prev_war["st"])
            ev = self.op("vector", lambda e: e.tensor_copy(self.wT_st[:, :], pst[:, :]))
            prev_war["pst"] = ev
            self.w("gpsimd", ev)
            st = self.dma("gpsimd", dst_kind.rearrange("c p t -> p c t"),
                          self.wT_st.ap().rearrange("p (c t) -> p c t", c=8))
            prev_war["st"] = st
            prev_wn[p] = v4
            return st

        for (dram, src) in ((self.d_wupT, self.dwup), (self.d_wgateT, self.dwgate)):
            for ib in range(24):
                dst = dram.ap()[ib]
                one(src.ap()[ib * 128:(ib + 1) * 128, :], dst, True)
        for hb in range(8):
            for ib4 in range(3):
                dst = self.d_wdownT.ap()[ib4 * 8:(ib4 + 1) * 8, :, hb * 128:(hb + 1) * 128]
                one(self.dwdown.ap()[hb * 128:(hb + 1) * 128, ib4 * 1024:(ib4 + 1) * 1024],
                    dst, True)
        self.dequant_done = [self.tok("dA"), self.tok("dB")]

    def p1_norms(self):
        NSQ = self.b_bf(9216, 4096)
        Sget = self.ps_s.ap()
        self.w("scalar", self.kf_done)
        self.w("scalar", self.xn_own_stored)
        self.w("scalar", self.ostate["xt_done"])
        self.wait("scalar", "T", self.semn["T"])
        self.wait("tensor", "V", self.semn["V"])
        self.wait("tensor", "S", self.semn["S"])
        for h in range(2):
            z = self.op("vector", lambda e: e.memset(self.acc[:, A_NORM[0]:A_NORM[1] + 1], 0.0))
            self.w("vector", z)
            for idx, (src, accc) in enumerate(((self.QF[h], A_NORM[0]),
                                               (self.KF[h], A_NORM[1]))):
                sq = self.op("scalar", lambda e, s=src: e.activation(NSQ, s, AF.Square))
                self.w("tensor", sq)
                prev_ev = None
                for c in range(16):
                    bank = Sget[:, (c % 2) * 512:(c % 2) * 512 + 512]
                    if prev_ev is not None and c >= 2:
                        self.w("tensor", prev_ev)
                    lhsT = self.ones128b.ap()
                    rhs = NSQ[:, c * 512:(c + 1) * 512]
                    tmm = self.op("tensor", lambda e, b=bank, l=lhsT, r=rhs:
                                  e.matmul(b, l, r, start=True, stop=True))
                    self.w("vector", tmm)
                    rv = self.op("vector", lambda e, b=bank: e.reduce_max(
                        self.ac(A_NORM[0] + 2), b, axis=AX.X))
                    self.w("vector", rv)
                    am = self.op("vector", lambda e, a=accc: e.tensor_tensor(
                        self.ac(a), self.ac(A_NORM[0] + 2), self.ac(a), ALU.max))
                    prev_ev = am
                self.wait("scalar", "T", self.semn["T"])
            self.w("vector", prev_ev)
            u = self.op("vector", lambda e: e.tensor_mul(
                self.ac(A_NORM[0] + 2), self.ac(A_NORM[0]), self.ac(A_NORM[1])))
            self.w("scalar", u)
            l1 = self.op("scalar", lambda e: e.activation(
                self.ac(A_NORM[0] + 3), self.ac(A_NORM[0] + 2), AF.Ln))
            self.w("scalar", l1)
            l2 = self.op("scalar", lambda e: e.activation(
                self.ac(A_NORM[0] + 2), self.ac(A_NORM[0] + 3), AF.Exp, scale=0.5))
            self.w("vector", l2)
            nm = self.op("vector", lambda e, h=h: e.tensor_scalar_mul(
                self.sc(self.SLOT_NEGM[h]), self.ac(A_NORM[0] + 2), -1.03))
            self.negm_done = nm

    def p1_xq(self):
        XQF = self.b_bf(0, 8192)
        LD = [self.b_f32(13312, 1024), self.b_f32(14336, 1024)]
        IQ = [self.b_i16(15360, 512), self.b_i16(15872, 512)]
        XQB = [self.b_bf(16384, 512), self.b_bf(16896, 512)]
        pst = self.ps_t.ap()
        if self.debug and (self.tapmask is None or "SC" in self.tapmask):
            self.w("sync", self.sc_x_ready)
            self.w("sync", self.negm_done)
            self.tap(self.SC.ap()[:, 0:12], name="SC")
        self.w("vector", self.sc_x_ready)
        self.wait("vector", "T", self.semn["T"])
        self.wait("sync", "T", self.semn["T"])
        self.wait("sync", "V", self.semn["V"])
        prev = [dict(ld=None, iq=None, bf=None), dict(ld=None, iq=None, bf=None)]
        prev_pst = self.tok("vector")
        prev_tr = [self.tok("tensor"), self.tok("tensor")]
        for i in range(TOWN // 128):
            b = i % 2
            if prev[b]["iq"] is not None:
                self.w("sync", prev[b]["iq"])
            ld = self.dma("sync", LD[b], self.d_xn_own.ap()[i * 128:(i + 1) * 128, :])
            self.wait("vector", "dA", ld[1])
            if i == 0:
                self.w("sync", self.xn_own_stored)
            v1 = self.op("vector", lambda e, b=b: e.tensor_scalar_mul(
                IQ[b], LD[b], self.sc(self.SLOT_RX)))
            self.w("vector", v1)
            self.w("vector", prev_tr[b])
            v2 = self.op("vector", lambda e, b=b: e.tensor_copy(XQB[b], IQ[b]))
            self.w("tensor", v2)
            self.w("tensor", prev_pst)
            for e8 in range(8):
                fn = (lambda e, b=b, e8=e8: e.transpose(
                    pst[:, e8 * 128:(e8 + 1) * 128],
                    XQB[b][:, e8 * 128:(e8 + 1) * 128], self.identb[:, :]))
                if e8 == 7:
                    t_tr = self.op("tensor", fn)
                else:
                    self.opn("tensor", fn)
            prev_tr[b] = t_tr
            self.w("vector", t_tr)
            dst = XQF.rearrange("p (c t) -> p c t", c=8)[:, :, i * 128:(i + 1) * 128]
            src = pst.rearrange("p (c t) -> p c t", c=8)
            ev = self.op("vector", lambda e, dst=dst, src=src: e.tensor_copy(dst, src))
            prev_pst = ev
            prev[b] = dict(ld=ld, iq=v1, bf=v2)
        self.xqf_done = prev_pst
        self.tap(XQF[:, 0:2048], tok=self.xqf_done, name="xqf")

    def p2_upgate(self):
        XQF = self.b_bf(0, 8192)
        CH = [self.b_f32(14336 + i * 1024, 1024) for i in range(6)]
        wub = [self.wq_sb.ap()[:, 0:1024], self.wq_sb.ap()[:, 1024:2048]]
        Sget = self.ps_s.ap()
        z = self.op("vector", lambda e: e.memset(
            self.acc[:, A_AMUP:A_AMG + 1], 0.0))
        self.w("vector", z)
        self.w("tensor", self.xqf_done)
        self.wait("tensor", "V", self.semn["V"])
        self.wait("scalar", "V", self.semn["V"])
        self.wait("scalar", "T", self.semn["T"])
        self.wait("gpsimd", "T", self.semn["T"])
        for t in self.dequant_done:
            self.w("gpsimd", t)
        prev_wload = [self.tok("tensor"), self.tok("tensor")]
        prev_bank_ev = [self.tok("vector")] * 4
        prev_st = [None, None]
        prev_am = None
        ch_read = [None, None]
        for wgt, (dsrc, ddst, amc) in enumerate(
                ((self.d_wupT, self.d_up, A_AMUP),
                 (self.d_wgateT, self.d_gate, A_AMG))):
            for ib in range(24):
                wb = ib % 2
                chb = ib % 2
                self.w("gpsimd", prev_wload[wb])
                ldw = self.dma("gpsimd", wub[wb].rearrange("p (c t) -> p c t", c=8),
                               dsrc.ap()[ib].rearrange("c p t -> p c t"))
                self.wait("tensor", "dB", ldw[1])
                if prev_st[chb] is not None:
                    for t in prev_st[chb]:
                        self.w("scalar", t)
                self.w("scalar", ch_read[chb])
                for t4 in range(4):
                    bank = Sget[:, t4 * 512:(t4 + 1) * 512]
                    if prev_bank_ev[t4] is not None:
                        self.w("tensor", prev_bank_ev[t4])
                    for e8 in range(8):
                        lhsT = wub[wb][:, e8 * 128:(e8 + 1) * 128]
                        rhs = XQF.rearrange("p (c t) -> p c t", c=8)[:, e8,
                                                                    t4 * 512:(t4 + 1) * 512]
                        fn = (lambda e, b=bank, l=lhsT, r=rhs, e8=e8:
                              e.matmul(b, l, r, start=(e8 == 0), stop=(e8 == 7)))
                        if e8 == 7:
                            tmm = self.op("tensor", fn)
                        else:
                            self.opn("tensor", fn)
                    self.w("scalar", tmm)
                    dst = self._ug_dst(chb, t4)
                    ev = self.op("scalar", lambda e, dst=dst, bank=bank:
                                 e.activation(dst, bank, AF.Copy,
                                              scale=self.sc(self.SLOT_SX)))
                    prev_bank_ev[t4] = ev
                self.w("vector", prev_bank_ev[3])
                if prev_am is not None:
                    self.w("vector", prev_am)
                r1 = self.op("vector", lambda e, chb=chb: e.reduce_max(
                    self.ac(12), CH[chb], axis=AX.X, apply_absolute_value=True))
                r2 = self.op("vector", lambda e, chb=chb: e.reduce_max(
                    self.ac(13), CH[chb + 2], axis=AX.X, apply_absolute_value=True))
                self.w("vector", r1)
                self.w("vector", r2)
                m1 = self.op("vector", lambda e, a=amc: e.tensor_tensor(
                    self.ac(a), self.ac(12), self.ac(a), ALU.max))
                m2 = self.op("vector", lambda e, a=amc: e.tensor_tensor(
                    self.ac(a), self.ac(13), self.ac(a), ALU.max))
                prev_am = m2
                ch_read[chb] = m2
                self.w("sync", prev_bank_ev[3])
                self.w("gpsimd", prev_bank_ev[3])
                st1 = self.dma("sync", ddst.ap()[ib, :, 0:1024], CH[chb])
                st2 = self.dma("gpsimd", ddst.ap()[ib, :, 1024:2048], CH[chb + 2])
                prev_st[chb] = [st1, st2]
                prev_wload[wb] = tmm
                if ib == 0:
                    self.tap(ddst.ap()[0], tok=(st2[0], st2[1]), name=f"ug{wgt}")
        self.upgate_stores = [self.tok("dA"), self.tok("dB")]
        self.ldo_cc1 = self.cc_all_reduce_max(1, [A_AMUP, A_AMG])

    def _ug_dst(self, chb, t4):
        CH = [self.b_f32(14336 + i * 1024, 1024) for i in range(6)]
        buf = CH[chb] if t4 < 2 else CH[chb + 2]
        return buf[:, (t4 % 2) * 512:(t4 % 2) * 512 + 512]

    def cc1_scalars(self):
        self.wait("vector", self.ldo_cc1[0], self.ldo_cc1[1])
        self.ccout_to_srow(8, ldo=self.ldo_cc1, k=2)
        sr = self.sr
        v1 = self.op("vector", lambda e: e.tensor_scalar(
            sr(10), sr(8), 1.0 / QMAXF, 1e-12, ALU.mult, ALU.max))
        v2 = self.op("vector", lambda e: e.tensor_scalar(
            sr(11), sr(9), 1.0 / QMAXF, 1e-12, ALU.mult, ALU.max))
        self.w("vector", v1)
        v3 = self.op("vector", lambda e: e.reciprocal(sr(16 + 0), sr(10)))
        self.w("vector", v2)
        v4 = self.op("vector", lambda e: e.reciprocal(sr(16 + 1), sr(11)))
        v5 = self.op("vector", lambda e: e.tensor_copy(sr(16 + 2), sr(11)))
        v6 = self.op("vector", lambda e: e.tensor_scalar_mul(sr(16 + 3), sr(11), -1.0))
        v7 = self.op("vector", lambda e: e.tensor_scalar_mul(sr(12), sr(11), 1.0 / 65536.0))
        for t in (v3, v4, v5, v6, v7):
            self.w("vector", t)
        self.sc_ug_ready = self.bcast_row(self.srow[0:1, 16:20], 4, self.SLOT_RU)

    def cc2_scalars(self):
        self.wait("vector", self.ldo_cc2[0], self.ldo_cc2[1])
        self.ccout_to_srow(8, ldo=self.ldo_cc2)
        sr = self.sr
        v1 = self.op("vector", lambda e: e.tensor_mul(sr(14), sr(8), sr(12)))
        self.w("vector", v1)
        v2 = self.op("vector", lambda e: e.tensor_scalar(
            sr(14), sr(14), 1.0 / QMAXF, 1e-12, ALU.mult, ALU.max))
        self.w("vector", v2)
        v3 = self.op("vector", lambda e: e.reciprocal(sr(15), sr(14)))
        self.w("vector", v3)
        v4 = self.op("vector", lambda e: e.tensor_mul(sr(16), sr(12), sr(15)))
        v5 = self.op("vector", lambda e: e.tensor_mul(sr(13), sr(14), sr(10)))
        self.w("vector", v4)
        self.sc_c3_ready = self.bcast_row(self.srow[0:1, 16:17], 1, self.SLOT_C3)
        self.w("vector", v5)

    def cc3_scalars(self):
        self.wait("vector", self.ldo_cc3[0], self.ldo_cc3[1])
        self.ccout_to_srow(8, ldo=self.ldo_cc3)
        sr = self.sr
        v1 = self.op("vector", lambda e: e.tensor_mul(sr(17), sr(8), sr(13)))
        self.w("vector", v1)
        v2 = self.op("vector", lambda e: e.tensor_scalar(
            sr(17), sr(17), 1.0 / QMAXF, 1e-12, ALU.mult, ALU.max))
        self.w("vector", v2)
        v3 = self.op("vector", lambda e: e.reciprocal(sr(18), sr(17)))
        self.w("vector", v3)
        v4 = self.op("vector", lambda e: e.tensor_mul(sr(19), sr(13), sr(18)))
        self.w("vector", v4)
        v5 = self.op("vector", lambda e: e.tensor_copy(sr(20), sr(17)))
        self.w("vector", v5)
        self.sc_c4_ready = self.bcast_row(self.srow[0:1, 19:21], 2, self.SLOT_C4)

    def flash_head(self, h, pump):
        KF, QF, VA = self.KF[h], self.QF[h], self.VA[h]
        PB = [self.b_bf(8192, 512), self.b_bf(8704, 512)]
        OG = self.b_bf(9216, 4096)
        OSC = self.b_f32(13312, 1024)
        odt = self.wnat.ap()
        odt2 = self.wnat2.ap()
        W = self.wsc.ap()
        negm = self.sc(self.SLOT_NEGM[h])
        psum_o = self.ps_o.ap()

        qk_tok = [None, None]
        exp_tok = [None, None]
        pv_tok = [None, None]
        evac_tok = self.tok("vector")
        self.wait("vector", "V", self.semn["V"])
        self.wait("vector", "T", self.semn["T"])
        self.opn("vector", lambda e: e.memset(self.ps_o[:, 0:258], 0.0))
        memset_tok = self.op("vector", lambda e: e.memset(self.ps_o[:, 512:770], 0.0))
        og_done = []

        for qt in range(NQT):
            for kb in range(33):
                r = kb % 2
                b4 = kb % 2
                if kb < 32:
                    self.w("tensor", exp_tok[r])
                    for kti in range(2):
                        kt = 2 * kb + kti
                        for hh in range(2):
                            sub = self.SREG[r][:, hh * 512 + kti * 256:
                                               hh * 512 + kti * 256 + 256]
                            lhsT = KF[hh * 64:(hh + 1) * 64, kt * 128:(kt + 1) * 128]
                            rhs = QF[hh * 64:(hh + 1) * 64, qt * 256:(qt + 1) * 256]
                            fn = (lambda e, s=sub, l=lhsT, rr=rhs:
                                  e.matmul(s, l, rr, start=True, stop=True))
                            if kti == 1 and hh == 1:
                                qk_tok[r] = self.op("tensor", fn)
                            else:
                                self.opn("tensor", fn)
                    self.w("scalar", qk_tok[r])
                    self.w("scalar", pv_tok[b4])
                    exp_tok[b4] = self.op("scalar", lambda e, b4=b4, r=r: e.activation(
                        PB[b4], self.SREG[r], AF.Exp, bias=negm))
                if kb >= 1:
                    bp = (kb - 1) % 2
                    self.w("tensor", exp_tok[bp])
                    if kb - 1 == 0:
                        self.w("tensor", memset_tok)
                    for kti in range(2):
                        kt = 2 * (kb - 1) + kti
                        for hh in range(2):
                            for qc in range(2):
                                lhsT = PB[bp][:, hh * 512 + kti * 256 + qc * 128:
                                              hh * 512 + kti * 256 + qc * 128 + 128]
                                rhs = VA[:, kt * 129:kt * 129 + 129]
                                o = self.O_AP[hh][qc]
                                fn = (lambda e, o=o, l=lhsT, rr=rhs:
                                      e.matmul(o, l, rr, start=False, stop=True))
                                if kti == 1 and hh == 1 and qc == 1:
                                    pv_tok[bp] = self.op("tensor", fn)
                                else:
                                    self.opn("tensor", fn)
            self.w("vector", pv_tok[0])
            self.w("vector", pv_tok[1])
            self.w("vector", evac_tok)
            self.opn("vector", lambda e: e.tensor_copy(
                OSC[:, 0:258], self.ps_o[:, 0:258]))
            raw = self.op("vector", lambda e: e.tensor_copy(
                OSC[:, 512:770], self.ps_o[:, 512:770]))
            self.w("vector", raw)
            self.opn("vector", lambda e: e.memset(self.ps_o[:, 0:258], 0.0))
            memset_tok = self.op("vector", lambda e: e.memset(self.ps_o[:, 512:770], 0.0))
            for qc in range(2):
                c00 = 0 * 512 + qc * 129
                c10 = 1 * 512 + qc * 129
                O0b = OSC[:, c00:c00 + 128]
                Z0 = OSC[:, c00 + 128:c00 + 129]
                O1b = OSC[:, c10:c10 + 128]
                Z1 = OSC[:, c10 + 128:c10 + 129]
                w0 = W[:, qc * 16:qc * 16 + 1]
                w1 = W[:, qc * 16 + 1:qc * 16 + 2]
                w2 = W[:, qc * 16 + 2:qc * 16 + 3]
                v1 = self.op("vector", lambda e, Z0=Z0, w0=w0: e.reciprocal(w0, Z0))
                v2 = self.op("vector", lambda e, Z1=Z1, w1=w1: e.reciprocal(w1, Z1))
                self.w("vector", v2)
                v3 = self.op("vector", lambda e, w1=w1, w2=w2: e.tensor_mul(
                    w2, w1, self.sc(self.SLOT_NEGLAM)))
                self.w("vector", v1)
                t0 = odt[:, qc * 512:qc * 512 + 128]
                od = odt[:, qc * 512 + 128:qc * 512 + 256]
                v4 = self.op("vector", lambda e, O0b=O0b, w0=w0, t0=t0:
                             e.tensor_scalar_mul(t0, O0b, w0))
                self.w("vector", v3)
                self.w("vector", v4)
                v5 = self.op("vector", lambda e, O1b=O1b, w2=w2, t0=t0, od=od:
                             e.scalar_tensor_tensor(od, O1b, w2, t0, ALU.mult, ALU.add))
                self.w("vector", v5)
                w3 = W[:, qc * 16 + 3:qc * 16 + 4]
                w4 = W[:, qc * 16 + 4:qc * 16 + 5]
                w5 = W[:, qc * 16 + 5:qc * 16 + 6]
                w6 = W[:, qc * 16 + 6:qc * 16 + 7]
                v6 = self.op("vector", lambda e, od=od, w3=w3: e.reduce_sum(w3, od, axis=AX.X))
                self.w("vector", v6)
                v7 = self.op("vector", lambda e, w3=w3, w4=w4: e.tensor_scalar_mul(w4, w3, -1.0 / 128.0))
                self.w("vector", v7)
                cc = odt[:, qc * 512 + 256:qc * 512 + 384]
                v8 = self.op("vector", lambda e, od=od, w4=w4, cc=cc: e.tensor_scalar_add(cc, od, w4))
                self.w("vector", v8)
                sq = odt[:, qc * 512 + 384:qc * 512 + 512]
                v9 = self.op("vector", lambda e, cc=cc, sq=sq: e.tensor_mul(sq, cc, cc))
                self.w("vector", v9)
                v10 = self.op("vector", lambda e, sq=sq, w5=w5: e.reduce_sum(w5, sq, axis=AX.X))
                self.w("vector", v10)
                v11 = self.op("vector", lambda e, w5=w5, w6=w6: e.tensor_scalar(
                    w6, w5, 1.0 / 128.0, LN_EPS, ALU.mult, ALU.add))
                self.w("scalar", v11)
                s1 = self.op("scalar", lambda e, w6=w6, w3=w3: e.activation(w3, w6, AF.Ln))
                self.w("scalar", s1)
                s2 = self.op("scalar", lambda e, w3=w3, w6=w6: e.activation(w6, w3, AF.Exp, scale=-0.5))
                self.w("vector", s2)
                cg = odt2[:, qc * 512:qc * 512 + 128]
                v12 = self.op("vector", lambda e, cc=cc, w6=w6, cg=cg:
                              e.scalar_tensor_tensor(cg, cc, w6, self.LNG.ap(), ALU.mult, ALU.mult))
                self.w("vector", v12)
                ogdst = OG[:, (qt * 2 + qc) * 128:(qt * 2 + qc) * 128 + 128]
                v13 = self.op("vector", lambda e, cg=cg, ogdst=ogdst: e.tensor_add(
                    ogdst, cg, self.LNB.ap()))
                og_done.append(v13)
            if qt == 0 and h == 0:
                self.tap(OSC, tok=raw, name="osc00")
                self.tap(OG[:, 0:256], tok=og_done[-1], name="og00")
            evac_tok = self.tok("vector")
            pump(h, qt)
        self.og_done = og_done[-1]
        return self.og_done

    def mini_transpose(self, h):
        OG = self.b_bf(9216, 4096)
        pst = self.ps_t.ap()
        dst_of = self.OFC[h]
        self.w("tensor", self.og_done)
        prev_pst = self.tok("vector")
        for g8 in range(8):
            self.w("tensor", prev_pst)
            for s in range(8):
                tt = g8 * 8 + s
                fn = (lambda e, tt=tt, s=s: e.transpose(
                    pst[:, s * 128:(s + 1) * 128],
                    OG[:, tt * 128:(tt + 1) * 128], self.identb[:, :]))
                if s == 7:
                    t_tr = self.op("tensor", fn)
                else:
                    self.opn("tensor", fn)
            self.w("vector", t_tr)
            ev = self.op("vector", lambda e, g8=g8: e.tensor_copy(
                dst_of[:, g8 * 1024:(g8 + 1) * 1024], pst[:, :]))
            prev_pst = ev
        self.ofc_done = prev_pst
        self.tap(dst_of[:, 0:2048], tok=self.ofc_done, name=f"ofc{h}")
        return prev_pst

    def tail(self):
        RSS = self.b_bf(19968, 512)
        Sget = self.ps_s.ap()
        A = self.AR_A.ap()
        WDT = self.b_bf(0, 12288)
        self.w("tensor", self.ofc_done)
        self.wait("tensor", "dB", self.ld_wo[1])
        self.wait("sync", "T", self.semn["T"])
        self.wait("sync", "V", self.semn["V"])
        self.wait("sync", "S", self.semn["S"])
        self.wait("sync", "dB", self.semn["dB"])
        ldw = self.dma("sync", WDT.rearrange("p (c t) -> p c t", c=24),
                       self.d_wdownT.ap().rearrange("c p t -> p c t"))
        self.tap(WDT[:, 0:4096], tok=ldw, name="wdt0")
        self.OQP = lambda tg, ib: A[:, 16384 + (tg * 24 + ib) * 512:
                                    16384 + (tg * 24 + ib) * 512 + 512]
        self.wait("gpsimd", "T", self.semn["T"])
        self.w("gpsimd", self.oq_stored)
        for tg in range(2):
            for ib in range(24):
                pl = self.dma("gpsimd", self.OQP(tg, ib),
                              self.d_oq.ap()[ib, :, tg * 512:(tg + 1) * 512])
        self.oq_preload = pl
        prev_ev = [None, None]
        prev_st = None
        for tt in range(64):
            pb = tt % 2
            bank0 = Sget[:, pb * 1024:pb * 1024 + 512]
            bank1 = Sget[:, pb * 1024 + 512:pb * 1024 + 1024]
            if prev_ev[pb] is not None:
                self.w("tensor", prev_ev[pb])
            for fh, bank in ((0, bank0), (1, bank1)):
                for h in range(2):
                    lhsT = self.OFC[h][:, tt * 128:(tt + 1) * 128]
                    rhs = self.wo_sb.ap()[:, h * 1024 + fh * 512:h * 1024 + fh * 512 + 512]
                    fn = (lambda e, b=bank, l=lhsT, r=rhs, h=h:
                          e.matmul(b, l, r, start=(h == 0), stop=(h == 1)))
                    if fh == 1 and h == 1:
                        tmm = self.op("tensor", fn)
                    else:
                        self.opn("tensor", fn)
            self.w("vector", tmm)
            if prev_st is not None:
                self.w("vector", prev_st)
            e1 = self.op("vector", lambda e, b=bank0: e.tensor_copy(RSS[:, 0:512], b))
            e2 = self.op("vector", lambda e, b=bank1: e.tensor_copy(RSS[:, 512:1024], b))
            prev_ev[pb] = e2
            self.w("sync", e2)
            st = self.dma("sync", self.d_rs_in.ap()[tt * 128:(tt + 1) * 128, :], RSS)
            prev_st = st
            if tt == 0:
                self.tap(self.d_rs_in.ap()[0:128, :], tok=st, name="rsin0")
        self.wait("gpsimd", "dA", prev_st[1])
        self.wait("gpsimd", "dB", self.oq_preload[1])
        self.wait("gpsimd", "dA", ldw[1])
        rin = self.d_rs_in.ap()
        rout = self.d_rs_out.ap()
        self.rs_tok = self.op("gpsimd", lambda e: e.collective_compute(
            "ReduceScatter", ALU.add, replica_groups=[[0, 1, 2, 3], [4, 5, 6, 7]],
            ins=[rin.opt()], outs=[rout.opt()]), sem="C")
        self.wait("sync", "C", self.rs_tok[1])
        if self.debug and (self.tapmask is None or "rsout0" in self.tapmask):
            self.tap(self.d_rs_out.ap()[0:128, :], name="rsout0")

        OQS = [self.b_bf(12288, 256), self.b_bf(12544, 256)]
        FF = [self.b_f32(12800 + i * 1024, 1024) for i in range(4)]
        XO = self.b_f32(16896, 1024)
        AT = self.b_f32(17920, 1024)
        OUTT = self.b_f32(18944, 1024)
        self.wait("tensor", "dA", ldw[1])
        self.wait("tensor", "dB", self.oq_preload[1])
        self.wait("tensor", "V", self.semn["V"])
        self.w("scalar", self.sc_c4_ready)
        self.wait("scalar", "V", self.semn["V"])
        self.wait("scalar", "dB", self.semn["dB"])
        prev_oqs = [self.tok("tensor"), self.tok("tensor")]
        out_dmas = []
        ff_read = [None, None, None, None]
        for tg in range(4):
            self.w("gpsimd", self.oq_stored)
            for fh in range(2):
                banks = [Sget[:, i * 512:(i + 1) * 512] for i in range(4)]
                for ib in range(24):
                    ob = ib % 2
                    if tg < 2:
                        src = self.OQP(tg, ib)
                    else:
                        self.w("gpsimd", prev_oqs[ob])
                        ldo = self.dma("gpsimd", OQS[ob],
                                       self.d_oq.ap()[ib, :, tg * 512:(tg + 1) * 512])
                        self.wait("tensor", "dB", ldo[1])
                        src = OQS[ob]
                    for tsub in range(4):
                        lhsT = src[:, tsub * 128:(tsub + 1) * 128]
                        rhs = WDT[:, ib * 1024 + fh * 512:ib * 1024 + fh * 512 + 512]
                        fn = (lambda e, b=banks[tsub], l=lhsT, r=rhs, ib=ib:
                              e.matmul(b, l, r, start=(ib == 0), stop=(ib == 23)))
                        if tsub == 3:
                            tmm = self.op("tensor", fn)
                        else:
                            self.opn("tensor", fn)
                    if tg >= 2:
                        prev_oqs[ob] = tmm
                self.w("scalar", tmm)
                evs = []
                for tsub in range(4):
                    self.w("scalar", ff_read[tsub])
                    ev = self.op("scalar", lambda e, b=banks[tsub], tsub=tsub, fh=fh:
                                 e.activation(FF[tsub][:, fh * 512:fh * 512 + 512],
                                              b, AF.Copy, scale=self.sc(self.SLOT_SO)))
                    evs.append(ev)
                for t in evs:
                    self.w("tensor", t)
                if tg == 0 and fh == 1:
                    self.tap(FF[0], tok=evs[0], name="ff0")
            for tsub in range(4):
                i = tg * 4 + tsub
                if out_dmas:
                    self.wait("sync", "V", self.semn["V"])
                ldx = self.dma("sync", XO, self.dx_own.ap()[i * 128:(i + 1) * 128, :])
                self.wait("gpsimd", "C", self.rs_tok[1])
                self.wait("gpsimd", "V", self.semn["V"])
                lda = self.dma("gpsimd", AT,
                               self.d_rs_out.ap()[i * 128:(i + 1) * 128, :])
                self.wait("vector", "dA", ldx[1])
                self.wait("vector", "dB", lda[1])
                self.w("scalar", evs[-1]) if False else None
                self.wait("vector", "S", self.semn["S"])
                if out_dmas:
                    self.wait("vector", "dA", out_dmas[-1][1])
                v1 = self.op("vector", lambda e, tsub=tsub: e.tensor_add(
                    OUTT, FF[tsub], XO))
                ff_read[tsub] = v1
                self.w("vector", v1)
                v2 = self.op("vector", lambda e: e.tensor_add(OUTT, OUTT, AT))
                self.w("sync", v2)
                if tg == 0 and tsub == 0:
                    self.tap(XO, name="xo0")
                    self.tap(AT, name="at0")
                    self.tap(OUTT, name="outt0")
                    self.tap(FF[0], name="ff0b")
                    self.tap(FF[1], name="ff1b")
                std = self.dma("sync", self.dout.ap()[i * 128:(i + 1) * 128, :], OUTT)
                out_dmas.append(std)
        self.final_out = out_dmas[-1]

    def chain_b_step(self, step):
        ib, hf = divmod(step, 2)
        CH = [self.b_f32(14336 + i * 1024, 1024) for i in range(6)]
        X, Z, X2, Z2, Wb = CH[0], CH[1], CH[2], CH[3], CH[4]
        Zi = self.b_i16(14336 + 1 * 1024, 512)
        Z2i = self.b_i16(14336 + 3 * 1024, 512)
        Z2u = self.b_u16(14336 + 3 * 1024, 512)
        st = self.cstate
        c0 = hf * 1024
        if step == 0:
            self.w("vector", self.sc_ug_ready)
            self.w("scalar", self.sc_ug_ready)
            for t in self.upgate_stores:
                self.w("sync", t)
                self.w("gpsimd", t)
                self.w("vector", t)
                self.w("scalar", t)
        self.w("sync", st.get("X_free"))
        ldu = self.dma("sync", X, self.d_up.ap()[ib, :, c0:c0 + 1024])
        self.wait("vector", ldu[0], ldu[1])
        if st.get("iu_store") is not None:
            self.w("vector", st["iu_store"])
        v_iu = self.op("vector", lambda e: e.tensor_scalar_mul(Zi, X, self.sc(self.SLOT_RU)))
        st["X_free"] = v_iu
        self.w("gpsimd", v_iu)
        st_iu = self.dma("gpsimd", self.d_iu.ap()[ib, :, c0:c0 + 1024], Zi)
        st["iu_store"] = st_iu
        self.w("gpsimd", st.get("X2_free"))
        ldg = self.dma("gpsimd", X2, self.d_gate.ap()[ib, :, c0:c0 + 1024])
        self.wait("vector", ldg[0], ldg[1])
        v_ig = self.op("vector", lambda e: e.tensor_scalar_mul(Z2i, X2, self.sc(self.SLOT_RG)))
        self.w("vector", v_ig)
        v_igf = self.op("vector", lambda e: e.tensor_copy(X2, Z2i))
        self.w("scalar", v_igf)
        if st.get("W_free") is not None:
            self.w("scalar", st["W_free"])
        s_t = self.op("scalar", lambda e: e.activation(Wb, X2, AF.Exp,
                                                       scale=self.sc(self.SLOT_RU + 3)))
        self.w("vector", s_t)
        self.w("vector", st_iu)
        v_s1 = self.op("vector", lambda e: e.tensor_scalar_add(Z, Wb, 1.0))
        self.w("vector", v_s1)
        v_rc = self.op("vector", lambda e: e.reciprocal(Wb, Z))
        self.w("vector", v_rc)
        v_s2 = self.op("vector", lambda e: e.tensor_scalar(Z2u, Wb, 65536.0, 65535.9,
                                                           ALU.mult, ALU.min))
        self.w("vector", v_s2)
        v_pr = self.op("vector", lambda e: e.tensor_tensor(Wb, X2, Z2u, ALU.mult))
        st["X2_free"] = v_pr
        self.w("vector", v_pr)
        v_am = self.op("vector", lambda e: e.reduce_max(self.ac(12), Wb, axis=AX.X,
                                                        apply_absolute_value=True))
        self.w("vector", v_am)
        v_m = self.op("vector", lambda e: e.tensor_tensor(
            self.ac(A_AMP1), self.ac(12), self.ac(A_AMP1), ALU.max))
        self.w("sync", v_pr)
        st_pr = self.dma("sync", self.d_prod.ap()[ib, :, c0:c0 + 1024], Wb)
        st["W_free"] = st_pr
        st["prod_max"] = v_m
        st["last_store"] = st_pr
        if step == 1:
            self.tap(self.d_iu.ap()[0], tok=st_iu, name="iu0")
            self.tap(self.d_prod.ap()[0], tok=st_pr, name="prod0")

    def chain_c_step(self, step):
        ib, hf = divmod(step, 2)
        CH = [self.b_f32(14336 + i * 1024, 1024) for i in range(6)]
        X, Z = CH[0], CH[1]
        Zi = self.b_i16(14336 + 1 * 1024, 512)
        Xi = self.b_i16(14336 + 0 * 1024, 512)
        W2 = CH[4]
        st = self.cstate
        c0 = hf * 1024
        self.w("sync", st.get("X_free"))
        ldp = self.dma("sync", X, self.d_prod.ap()[ib, :, c0:c0 + 1024])
        self.wait("vector", ldp[0], ldp[1])
        self.w("vector", st.get("Z_free"))
        if st.get("it_store") is not None:
            self.w("vector", st["it_store"])
        v_it = self.op("vector", lambda e: e.tensor_scalar_mul(Zi, X, self.sc(self.SLOT_C3)))
        st["X_free"] = v_it
        self.w("sync", v_it)
        st_it = self.dma("sync", self.d_it.ap()[ib, :, c0:c0 + 1024], Zi)
        st["it_store"] = st_it
        self.w("gpsimd", v_it)
        ldiu = self.dma("gpsimd", Xi, self.d_iu.ap()[ib, :, c0:c0 + 1024])
        self.wait("vector", ldiu[0], ldiu[1])
        self.w("vector", st.get("W_free"))
        v_p2 = self.op("vector", lambda e: e.tensor_tensor(W2, Zi, Xi, ALU.mult))
        self.w("vector", v_p2)
        st["X_free"] = v_p2
        st["Z_free"] = v_p2
        v_am = self.op("vector", lambda e: e.reduce_max(self.ac(12), W2, axis=AX.X,
                                                        apply_absolute_value=True))
        st["W_free"] = v_am
        self.w("vector", v_am)
        v_m = self.op("vector", lambda e: e.tensor_tensor(
            self.ac(A_AMP2), self.ac(12), self.ac(A_AMP2), ALU.max))
        st["prod2_max"] = v_m
        if step == 1:
            self.tap(self.d_it.ap()[0], tok=st_it, name="it0")

    def chain_d_step(self, step):
        ib, hf = divmod(step, 2)
        CH = [self.b_f32(14336 + i * 1024, 1024) for i in range(6)]
        W2 = CH[4]
        Xi = self.b_i16(14336 + 0 * 1024, 512)
        Zi = self.b_i16(14336 + 1 * 1024, 512)
        IO16 = self.b_i16(14336 + 2 * 1024, 512)
        OB = self.b_bf(14336 + 5 * 1024, 512)
        st = self.cstate
        c0 = hf * 1024
        self.w("sync", st.get("X_free"))
        ldt = self.dma("sync", Xi, self.d_it.ap()[ib, :, c0:c0 + 1024])
        self.w("gpsimd", st.get("Z_free"))
        ldu = self.dma("gpsimd", Zi, self.d_iu.ap()[ib, :, c0:c0 + 1024])
        self.wait("vector", ldt[0], ldt[1])
        self.wait("vector", ldu[0], ldu[1])
        self.w("vector", st.get("W_free"))
        v_p2 = self.op("vector", lambda e: e.tensor_tensor(W2, Xi, Zi, ALU.mult))
        st["X_free"] = v_p2
        st["Z_free"] = v_p2
        self.w("vector", v_p2)
        self.w("vector", st.get("X2_free"))
        v_io = self.op("vector", lambda e: e.tensor_scalar_mul(IO16, W2, self.sc(self.SLOT_C4)))
        st["W_free"] = v_io
        st["X2_free"] = None
        self.w("vector", v_io)
        if st.get("oq_store") is not None:
            self.w("vector", st["oq_store"])
        v_ob = self.op("vector", lambda e: e.tensor_copy(OB, IO16))
        st["X2_free"] = v_ob
        self.w("sync", v_ob)
        st_oq = self.dma("sync", self.d_oq.ap()[ib, :, c0:c0 + 1024], OB)
        st["oq_store"] = st_oq
        self.oq_stored = st_oq
        if step == 1:
            self.tap(self.d_oq.ap()[0], tok=st_oq, name="oq0")

    def build(self):
        self.cstate = {}
        self.p0()
        self.p1_full()
        self.p1_norms()
        self.cc0_scalars()
        self.p1_dequant()
        self.p1_xq()
        self.p2_upgate()

        self.ldo_cc2 = None
        self.ldo_cc3 = None
        self._bstep = 0
        self._cstep = 0
        self._dstep = 0

        def pump0(h, qt):
            if qt == 0:
                self.cc1_scalars()
            elif qt < 24:
                for _ in range(2):
                    if self._bstep < 48:
                        self.chain_b_step(self._bstep)
                        self._bstep += 1
            elif qt == 24:
                while self._bstep < 48:
                    self.chain_b_step(self._bstep)
                    self._bstep += 1
                self.w("vector", self.cstate["prod_max"])
                self.ldo_cc2 = self.cc_all_reduce_max(2, [A_AMP1])
            elif qt == 26:
                self.cc2_scalars()

        def pump1(h, qt):
            if qt < 16:
                for _ in range(3):
                    if self._cstep < 48:
                        self.chain_c_step(self._cstep)
                        self._cstep += 1
            elif qt == 16:
                while self._cstep < 48:
                    self.chain_c_step(self._cstep)
                    self._cstep += 1
                self.w("vector", self.cstate["prod2_max"])
                self.ldo_cc3 = self.cc_all_reduce_max(3, [A_AMP2])
            elif qt == 18:
                self.cc3_scalars()
            elif qt >= 20:
                for _ in range(4):
                    if self._dstep < 48:
                        self.chain_d_step(self._dstep)
                        self._dstep += 1

        self.w("scalar", self.negm_done)
        self.wait("tensor", "V", self.semn["V"])
        self.wait("tensor", "S", self.semn["S"])
        self.flash_head(0, pump0)
        self.mini_transpose(0)
        self.flash_head(1, pump1)
        while self._dstep < 48:
            self.chain_d_step(self._dstep)
            self._dstep += 1
        if self.debug and (self.tapmask is None or "SCfinal" in self.tapmask):
            self.w("sync", self.sc_c4_ready)
            self.tap(self.SC.ap()[:, 0:12], name="SCfinal")
            self.tap(self.srow.ap(), name="srow")
        self.mini_transpose(1)
        self.tail()

        for eng in ("sync", "gpsimd", "tensor", "scalar", "vector"):
            self.wait(eng, "dA", self.semn["dA"])
            self.wait(eng, "dB", self.semn["dB"])
            self.wait(eng, "C", self.semn["C"])
        return self

    def check_deadlock(self):
        streams = {e: [t for t in self.trace if t[1] == e]
                   for e in ("sync", "gpsimd", "tensor", "scalar", "vector")}
        pos = {e: 0 for e in streams}
        sems = {s: 0 for s in self.semn}
        progress = True
        while progress:
            progress = False
            for e, ops_ in streams.items():
                while pos[e] < len(ops_):
                    kind, _, sem, val = ops_[pos[e]]
                    if kind == "wait" and sems[sem] < val:
                        break
                    if kind == "inc":
                        sems[sem] += val
                    pos[e] += 1
                    progress = True
        stuck = {e: (pos[e], len(streams[e])) for e in streams if pos[e] < len(streams[e])}
        if stuck:
            for e in stuck:
                kind, _, sem, val = streams[e][pos[e]]
                print(f"DEADLOCK: {e} at {pos[e]}/{len(streams[e])} waiting {sem}>={val} (now {sems[sem]})")
            return False
        print("deadlock check: OK")
        return True

    def finalize(self):
        with self.nc.Block() as block:
            plans = self.ops

            @block.sync
            def _(eng):
                for f in plans["sync"]:
                    f(eng)

            @block.gpsimd
            def _(eng):
                for f in plans["gpsimd"]:
                    f(eng)

            @block.tensor
            def _(eng):
                for f in plans["tensor"]:
                    f(eng)

            @block.scalar
            def _(eng):
                for f in plans["scalar"]:
                    f(eng)

            @block.vector
            def _(eng):
                for f in plans["vector"]:
                    f(eng)
        return self.nc


LAST_KB = None


def build_kernel(debug=False):
    global LAST_KB
    kb = KB(debug=debug)
    kb.build()
    nc = kb.finalize()
    LAST_KB = kb
    return nc


def kernel(**inputs):
    x = np.ascontiguousarray(np.asarray(inputs["L0_attn_x"], dtype=np.float32))
    Wq = np.asarray(inputs["L0_attn_Wq"], dtype=np.float32)
    Wk = np.asarray(inputs["L0_attn_Wk"], dtype=np.float32)
    Wv = np.asarray(inputs["L0_attn_Wv"], dtype=np.float32)
    Wo = np.asarray(inputs["L0_attn_Wo"], dtype=np.float32)
    lq1 = np.asarray(inputs["L0_attn_lq1"], dtype=np.float32)
    lk1 = np.asarray(inputs["L0_attn_lk1"], dtype=np.float32)
    lq2 = np.asarray(inputs["L0_attn_lq2"], dtype=np.float32)
    lk2 = np.asarray(inputs["L0_attn_lk2"], dtype=np.float32)
    ln_g = np.asarray(inputs["L0_attn_ln_g"], dtype=np.float32)
    ln_b = np.asarray(inputs["L0_attn_ln_b"], dtype=np.float32)
    wg = np.ascontiguousarray(np.asarray(inputs["L0_ffn_w_gate"], dtype=np.float32))
    wu = np.ascontiguousarray(np.asarray(inputs["L0_ffn_w_up"], dtype=np.float32))
    wd = np.ascontiguousarray(np.asarray(inputs["L0_ffn_w_down"], dtype=np.float32))
    lvecs = np.concatenate([lq1, lk1, lq2, lk2]).reshape(1, 256)

    nc = build_kernel(debug=False)
    in_maps = []
    for c in range(8):
        b, j = c // 4, c % 4
        in_maps.append({
            "x_full": np.ascontiguousarray(x[b]),
            "x_own": np.ascontiguousarray(x[b, j * TOWN:(j + 1) * TOWN]),
            "wq_sl": np.ascontiguousarray(Wq[:, 256 * j:256 * (j + 1)]),
            "wk_sl": np.ascontiguousarray(Wk[:, 256 * j:256 * (j + 1)]),
            "wv_sl": np.ascontiguousarray(Wv[:, 256 * j:256 * (j + 1)]),
            "wo_sl": np.ascontiguousarray(Wo[256 * j:256 * (j + 1), :]),
            "w_gate": wg, "w_up": wu, "w_down": wd,
            "lvecs": lvecs,
            "ln_g": ln_g.reshape(1, 128), "ln_b": ln_b.reshape(1, 128),
        })
    res = run_bass_kernel_spmd(nc, in_maps, core_ids=list(range(8)))
    out = np.empty((B, NTOK, E), np.float32)
    for c in range(8):
        b, j = c // 4, c % 4
        out[b, j * TOWN:(j + 1) * TOWN] = res.results[c]["out_own"]
    return out
